# version 02
# baseline (speedup 1.0000x reference)
import numpy as np
import ml_dtypes
import concourse.bass as bass
import concourse.mybir as mybir
from concourse.bass_utils import run_bass_kernel_spmd

F32 = mybir.dt.float32
BF16 = mybir.dt.bfloat16
I8 = mybir.dt.int8
AF = mybir.ActivationFunctionType
ALU = mybir.AluOpType
AX = mybir.AxisListType
bfdt = ml_dtypes.bfloat16

B, S, H = 2, 8192, 2048
NH, D = 16, 128
T = 2048
NCH = 16
C = 128
INTER = 8192
SCALE_BASE = 512.0
GN_EPS = 1e-5
LN_EPS = 1e-5
N_CORES = 8
NQ = 4
NTB = 32
NIB = 256
NSTRIP = 4
STRIP = 512
DBGH = 8
SERIALIZE = False



def _xpos_tables_f64(d, positions):
    half = d // 2
    xscale = (np.arange(0, d, 2, dtype=np.float64) + 0.4 * d) / (1.4 * d)
    pos = positions.astype(np.float64)
    scale = xscale[None, :] ** (pos[:, None] / SCALE_BASE)
    inv_freq = 1.0 / 10000.0 ** (np.arange(half, dtype=np.float64) / half)
    sinu = pos[:, None] * inv_freq[None, :]
    sin, cos = np.sin(sinu), np.cos(sinu)
    dup = lambda t: np.repeat(t, 2, axis=-1)
    return dup(cos * scale), dup(sin * scale), dup(cos / scale), dup(sin / scale)


def _build_host_constants():
    gammas = 1.0 - np.exp(np.linspace(np.log(1 / 32), np.log(1 / 512), NH))
    p = np.arange(T)
    cq, sq, ck, sk = _xpos_tables_f64(D, p)
    sgn = np.ones(D)
    sgn[0::2] = -1.0
    sq = sq * sgn[None, :]
    sk = sk * sgn[None, :]

    def to_col(tab):
        return np.ascontiguousarray(
            tab.reshape(NCH, C, D).transpose(1, 0, 2).reshape(C, NCH * D))

    cq_t = to_col(cq).astype(bfdt)
    sq_t = to_col(sq).astype(bfdt)
    ck_t = to_col(ck).astype(bfdt)
    sk_t = to_col(sk).astype(bfdt)

    pc = (np.arange(NCH)[None, :] * C + np.arange(C)[:, None] + 1.0)
    gq = np.empty((C, NH * NCH), np.float64)
    gk = np.empty((C, NH * NCH), np.float64)
    for n in range(NH):
        gq[:, n * NCH:(n + 1) * NCH] = gammas[n] ** pc
        gk[:, n * NCH:(n + 1) * NCH] = gammas[n] ** (-pc)
    gq = gq.astype(np.float32)
    gk = gk.astype(np.float32)

    causal = (np.arange(C)[None, :] >= np.arange(C)[:, None]).astype(np.float64)
    causal_rep = np.ascontiguousarray(causal.astype(bfdt))
    ident = np.ascontiguousarray(np.eye(C, dtype=bfdt))

    half = D // 2
    xscale = (np.arange(0, D, 2, dtype=np.float64) + 0.4 * D) / (1.4 * D)
    inv_freq = 1.0 / 10000.0 ** (np.arange(half, dtype=np.float64) / half)

    def that_lhsT(D_off, n):
        sig = xscale ** (D_off / SCALE_BASE)
        th = D_off * inv_freq
        g = float(np.float32(gammas[n] ** D_off))
        M = np.zeros((D, D))
        for i in range(half):
            cth, sth = np.cos(th[i]), np.sin(th[i])
            M[2 * i, 2 * i] = cth * sig[i] * g
            M[2 * i, 2 * i + 1] = sth * sig[i] * g
            M[2 * i + 1, 2 * i] = -sth * sig[i] * g
            M[2 * i + 1, 2 * i + 1] = cth * sig[i] * g
        return M.T

    that_by_core = []
    for core in range(N_CORES):
        bq, qq = core // NQ, core % NQ
        that = np.zeros((N_CORES, NH, D, D), np.float64)
        for r in range(N_CORES):
            rb, rq = r // NQ, r % NQ
            if rb == bq and rq < qq:
                for n in range(NH):
                    that[r, n] = that_lhsT(T * (qq - rq), n)
        that_by_core.append(np.ascontiguousarray(that.astype(bfdt)))

    return dict(cq=cq_t, sq=sq_t, ck=ck_t, sk=sk_t, gq=gq, gk=gk,
                causal=causal_rep, ident=ident, that_by_core=that_by_core)



class Buf:
    __slots__ = ("writer", "readers", "sem", "group", "writers")

    def __init__(self, sem=None, group=False):
        self.writer = None
        self.writers = []
        self.readers = []
        self.sem = sem
        self.group = group


class Sched:
    ENGINES = ("te", "act", "dve", "pool", "sp")

    def __init__(self):
        self.ops = []

    def add(self, eng, fn, reads=(), writes=(), deps=(), dma=False, dsem=None):
        idx = len(self.ops)
        dep = set(deps)
        if SERIALIZE and idx > 0:
            dep.add(idx - 1)
        for b in reads:
            if b.group:
                dep.update(b.writers)
            elif b.writer is not None:
                dep.add(b.writer)
        for b in writes:
            if b.group:
                dep.update(b.readers)
                b.writers.append(idx)
            else:
                if b.writer is not None:
                    dep.add(b.writer)
                dep.update(b.readers)
                b.writer = idx
                b.readers = []
        dep.discard(idx)
        for d in dep:
            assert d < idx
        if dma:
            if dsem is None and writes:
                dsem = writes[0].sem
            assert dsem is not None, "dma op needs a dsem"
        self.ops.append(dict(i=idx, eng=eng, fn=fn, deps=sorted(dep), dma=dma,
                             dsem=dsem))
        for b in reads:
            b.readers.append(idx)
        for b in writes:
            if not b.group:
                pass
        return idx

    def finalize(self):
        needed = set()
        for op in self.ops:
            needed.update(op["deps"])
        counters = {}
        pos = {}
        for op in self.ops:
            op["needed"] = op["i"] in needed or op["dma"]
            if op["needed"]:
                key = ("d", op["dsem"]) if op["dma"] else ("c", op["eng"])
                inc = 16 if op["dma"] else 1
                counters[key] = counters.get(key, 0) + inc
                pos[op["i"]] = (key, counters[key])
                op["inc"] = inc
        clocks = {e: {} for e in self.ENGINES}
        for op in self.ops:
            wmap = {}
            for d in op["deps"]:
                dop = self.ops[d]
                key, tgt = pos[d]
                if (not dop["dma"]) and dop["eng"] == op["eng"] == "te":
                    continue
                if wmap.get(key, 0) < tgt:
                    wmap[key] = tgt
            clk = clocks[op["eng"]]
            ww = []
            for key in sorted(wmap):
                tgt = wmap[key]
                if clk.get(key, 0) >= tgt:
                    continue
                clk[key] = tgt
                ww.append((key, tgt))
            op["waits"] = ww

    def sem_keys(self):
        ks = set()
        for op in self.ops:
            if op.get("needed"):
                ks.add(("d", op["dsem"]) if op["dma"] else ("c", op["eng"]))
        return sorted(ks)

    def emit_engine(self, eng_name, eng_obj, sems):
        for op in self.ops:
            if op["eng"] != eng_name:
                continue
            for key, tgt in op["waits"]:
                eng_obj.wait_ge(sems[key], tgt)
            inst = op["fn"](eng_obj)
            if op["needed"]:
                assert inst is not None, f"op {op['i']} on {eng_name} returned None"
                key = ("d", op["dsem"]) if op["dma"] else ("c", op["eng"])
                inst.then_inc(sems[key], op["inc"])



def build_program(stop_after=None):
    nc = bass.Bass("TRN2", target_bir_lowering=False)

    dr = {}
    ei = "ExternalInput"
    dr["x"] = nc.dram_tensor("x", [T, H], F32, kind=ei)
    for nm in ("wq", "wk", "wv", "wg", "wo"):
        dr[nm] = nc.dram_tensor(nm, [H, H], F32, kind=ei)
    dr["gnw"] = nc.dram_tensor("gnw", [C, H], F32, kind=ei)
    dr["gnb"] = nc.dram_tensor("gnb", [C, H], F32, kind=ei)
    for nm in ("cq", "sq", "ck", "sk"):
        dr[nm] = nc.dram_tensor(nm, [C, T], BF16, kind=ei)
    dr["causal"] = nc.dram_tensor("causal", [C, C], BF16, kind=ei)
    dr["gq"] = nc.dram_tensor("gq", [C, NH * NCH], F32, kind=ei)
    dr["gk"] = nc.dram_tensor("gk", [C, NH * NCH], F32, kind=ei)
    dr["ident"] = nc.dram_tensor("ident", [C, C], BF16, kind=ei)
    dr["that"] = nc.dram_tensor("that", [N_CORES, NH, D, D], BF16, kind=ei)
    dr["wgt"] = nc.dram_tensor("wgt", [NTB, 16, C, NIB], I8, kind=ei)
    dr["wut"] = nc.dram_tensor("wut", [NTB, 16, C, NIB], I8, kind=ei)
    dr["wdt"] = nc.dram_tensor("wdt", [INTER, H], I8, kind=ei)
    dr["sgate"] = nc.dram_tensor("sgate", [C, INTER // C], F32, kind=ei)
    dr["sup"] = nc.dram_tensor("sup", [C, INTER // C], F32, kind=ei)
    dr["sdown"] = nc.dram_tensor("sdown", [C, H], F32, kind=ei)
    dr["y"] = nc.dram_tensor("y", [T, H], F32, kind="ExternalOutput")
    dr["vsp"] = nc.dram_tensor("vsp", [NH, C, T], BF16, kind="Internal")
    dr["qsp"] = nc.dram_tensor("qsp", [NH, C, T], BF16, kind="Internal")
    dr["gsp"] = nc.dram_tensor("gsp", [NH, C, T], BF16, kind="Internal")
    dr["ztsp"] = nc.dram_tensor("ztsp", [NH, C, T], BF16, kind="Internal")
    dr["y1sp"] = nc.dram_tensor("y1sp", [NCH, C, H], F32, kind="Internal")
    dr["ccin"] = nc.dram_tensor("ccin", [C, NH * D], BF16, kind="Internal")
    dr["ccout"] = nc.dram_tensor("ccout", [N_CORES * C, NH * D], BF16, kind="Internal")

    A = {k: v.ap() for k, v in dr.items()}

    XT = nc.alloc_sbuf_tensor("XTa", [128, 16 * T], BF16)
    KT = nc.alloc_sbuf_tensor("KTa", [128, 16 * T], BF16)
    COLS = nc.alloc_sbuf_tensor("COLSa", [128, 8 * T], BF16)
    F32A = nc.alloc_sbuf_tensor("F32Aa", [128, 3 * H], F32)
    TAB = nc.alloc_sbuf_tensor("TABa", [128, 2 * T], BF16)
    CAU = nc.alloc_sbuf_tensor("CAUa", [128, C], BF16)
    GNBB = nc.alloc_sbuf_tensor("GNBBa", [128, H], BF16)
    GAM = nc.alloc_sbuf_tensor("GAMa", [128, 2 * NH * NCH], F32)
    IDN = nc.alloc_sbuf_tensor("IDNa", [128, C], BF16)
    GUB = nc.alloc_sbuf_tensor("GUBa", [128, 2 * STRIP], BF16)
    ST = nc.alloc_sbuf_tensor("STa", [128, 120], F32)
    SNB = nc.alloc_sbuf_tensor("SNBa", [128, T], BF16)
    SGC = nc.alloc_sbuf_tensor("SGCa", [128, 2 * (INTER // C)], F32)

    xt = XT.ap(); kt = KT.ap(); cols = COLS.ap(); f32a = F32A.ap()
    tab = TAB.ap(); cau = CAU.ap(); gnbb = GNBB.ap(); gam = GAM.ap()
    idn = IDN.ap(); gub = GUB.ap(); st = ST.ap(); snb = SNB.ap(); sgc = SGC.ap()
    tabf = tab.bitcast(F32)

    PA = nc.alloc_psum_tensor("PAa", [128, 2048], F32)
    PB = nc.alloc_psum_tensor("PBa", [128, 2048], F32)
    pa = PA.ap(); pb = PB.ap()
    pbb = pb.bitcast(BF16)

    Sc = Sched()
    bXT = [Buf(sem="xt") for _ in range(16)]
    bKT = [Buf() for _ in range(16)]
    bCOL = [Buf(sem=f"cols{i}") for i in range(8)]
    bF32 = [Buf(sem=f"f32{i}") for i in range(3)]
    bTAB = Buf(sem="tab"); bCAU = Buf(sem="cau"); bGNB = Buf(sem="gnb")
    bGAM = Buf(sem="gam"); bIDN = Buf(sem="idn")
    bGUB = [Buf(), Buf()]; bST = Buf(); bSNB = Buf(sem="snb"); bSGC = Buf(sem="sgc")
    bBank = [Buf() for _ in range(8)]
    bDR = {k: Buf(sem=k, group=True)
           for k in ("vsp", "qsp", "gsp", "ztsp", "y1sp")}
    bDR["ccin"] = Buf(sem="ccin")
    bDR["ccout"] = Buf()

    banksA = [bBank[i] for i in range(4)]
    banksB = [bBank[i] for i in range(4, 8)]
    bTRhi = [bBank[6], bBank[7]]
    bTRlo = [bBank[4], bBank[5]]

    cs = lambda i: cols[:, i * T:(i + 1) * T]
    fs = lambda i: f32a[:, i * H:(i + 1) * H]
    sti = lambda i: st[:, i * 20:i * 20 + 16]
    st1 = lambda i: st[:, i * 20:i * 20 + 1]

    eps_col = st[:, 100:101]
    Sc_eps_done = []

    def dma(engq, out, in_, reads=(), writes=(), deps=(), dsem=None):
        return Sc.add(engq, lambda e, o=out, i=in_: e.dma_start(out=o, in_=i),
                      reads=reads, writes=writes, deps=deps, dma=True, dsem=dsem)

    def finish():
        from contextlib import ExitStack
        Sc.finalize()
        with ExitStack() as stack:
            sems = {}
            for key in Sc.sem_keys():
                nm = ("cs_" + key[1]) if key[0] == "c" else ("ds_" + str(key[1]))
                sems[key] = stack.enter_context(nc.semaphore(nm))
            block = stack.enter_context(nc.Block())

            @block.tensor
            def _(eng):
                Sc.emit_engine("te", eng, sems)

            @block.scalar
            def _(eng):
                Sc.emit_engine("act", eng, sems)

            @block.vector
            def _(eng):
                Sc.emit_engine("dve", eng, sems)

            @block.gpsimd
            def _(eng):
                Sc.emit_engine("pool", eng, sems)

            @block.sync
            def _(eng):
                Sc.emit_engine("sp", eng, sems)
        return nc

    def dbg_sbuf(nm, ap, bufs, dtype=BF16, width=16 * T):
        dt_ = nc.dram_tensor(nm, [128, width], dtype, kind="ExternalOutput")
        dd = dma("sp", dt_.ap()[:, :], ap, reads=bufs, writes=[], dsem="out")
        Sc.add("sp", lambda e: e.nop(), deps=[dd])

    def dbg_dram(nm, key, shape, dtype):
        dt_ = nc.dram_tensor(nm, shape, dtype, kind="ExternalOutput")
        idx = tuple(slice(None) for _ in shape)
        dd = dma("sp", dt_.ap()[idx], A[key][idx], reads=[bDR[key]], writes=[], dsem="out")
        Sc.add("sp", lambda e: e.nop(), deps=[dd])

    Sc.add("dve", lambda e: e.memset(eps_col, 1e-5), writes=[bST])
    dma("sp", cau[:, :], A["causal"][:, :], writes=[bCAU])
    dma("sp", idn[:, :], A["ident"][:, :], writes=[bIDN])
    dma("sp", gam[:, 0:NH * NCH], A["gq"][:, :], writes=[bGAM])
    dma("sp", gam[:, NH * NCH:2 * NH * NCH], A["gk"][:, :], writes=[bGAM])
    dma("sp", tab[:, 0:T], A["ck"][:, :], writes=[bTAB])
    dma("sp", tab[:, T:2 * T], A["sk"][:, :], writes=[bTAB])

    for c_ in range(NCH):
        xst = fs(0)
        dma("sp", xst, A["x"][c_ * C:(c_ + 1) * C, :], reads=[bF32[0]],
            writes=[bF32[0]])
        xb = cs(0)
        Sc.add("act", lambda e, o=xb, i=xst: e.copy(o, i),
               reads=[bF32[0]], writes=[bCOL[0]])

        def mk_tp(src=xb):
            def f(e):
                last = None
                for ht in range(16):
                    last = nc.tensor.matmul(
                        pbb[:, 2048 + ht * 128:2048 + (ht + 1) * 128],
                        src[:, ht * 128:(ht + 1) * 128], idn[:, :],
                        start=(ht % 8 == 0), stop=(ht % 8 == 7),
                        is_transpose=True, skip_group_check=True)
                return last
            return f
        Sc.add("te", mk_tp(), reads=[bCOL[0], bIDN], writes=bTRhi)
        dst = xt.rearrange("p (ht t) -> p ht t", ht=16)[:, :, c_ * C:(c_ + 1) * C]
        src = pbb[:, 2048:4096].rearrange("p (ht a) -> p ht a", ht=16)
        Sc.add("act", lambda e, o=dst, i=src: e.copy(o, i),
               reads=bTRhi, writes=list(bXT))

    if stop_after == "p0":
        dbg_sbuf("d_xt", xt[:, :], list(bXT))
        return finish()

    def proj_tile(n, wap, wbuf, wapb, out_ap, out_buf):
        dma("pool", wapb.rearrange("p (ht d) -> p ht d", ht=16),
            wap[:, n * D:(n + 1) * D].rearrange("(ht p) d -> p ht d", p=128),
            reads=[wbuf], writes=[wbuf])
        for half in range(2):
            def mk_mm(half=half, wapb=wapb):
                def f(e):
                    last = None
                    for ht in range(16):
                        for j in range(2):
                            off = ht * T + half * 1024 + j * 512
                            last = nc.tensor.matmul(
                                pb[:, j * 512:(j + 1) * 512],
                                wapb[:, ht * D:(ht + 1) * D],
                                xt[:, off:off + 512],
                                start=(ht == 0), stop=(ht == 15),
                                skip_group_check=True)
                    return last
                return f
            Sc.add("te", mk_mm(), reads=[wbuf] + list(bXT), writes=bTRlo)
            Sc.add("act",
                   lambda e, o=out_ap[:, half * 1024:(half + 1) * 1024],
                   i=pb[:, 0:1024]: e.copy(o, i),
                   reads=bTRlo, writes=[out_buf])

    def transpose16(src_ap, src_buf, dst_ap, dst_bufs, region="hi"):
        reg_off = 2048 if region == "hi" else 0
        reg_bufs = bTRhi if region == "hi" else bTRlo

        def f(e, src_ap=src_ap, reg_off=reg_off):
            last = None
            for c_ in range(16):
                last = nc.tensor.matmul(
                    pbb[:, reg_off + c_ * 128:reg_off + (c_ + 1) * 128],
                    src_ap[:, c_ * 128:(c_ + 1) * 128], idn[:, :],
                    start=(c_ % 8 == 0), stop=(c_ % 8 == 7),
                    is_transpose=True, skip_group_check=True)
            return last
        Sc.add("te", f, reads=[src_buf, bIDN], writes=reg_bufs)
        Sc.add("act",
               lambda e, o=dst_ap, i=pbb[:, reg_off:reg_off + 2048]: e.copy(o, i),
               reads=reg_bufs, writes=list(dst_bufs))

    def xpos_apply(col_ap, col_buf, gam_off, out_ap, out_bufs):
        tmp = fs(0)
        sw = col_ap.rearrange("p (c k two) -> p c k two", c=NCH, two=2)[:, :, :, ::-1]
        stv = tab[:, T:2 * T].rearrange("p (c k two) -> p c k two", c=NCH, two=2)
        Sc.add("dve", lambda e, o=tmp, i0=sw, i1=stv:
               e.tensor_tensor(o.rearrange("p (c k two) -> p c k two", c=NCH, two=2),
                               i0, i1, ALU.mult),
               reads=[col_buf, bTAB], writes=[bF32[0]])
        Sc.add("dve", lambda e, o=col_ap, i1=tab[:, 0:T]:
               e.tensor_tensor(o, o, i1, ALU.mult),
               reads=[col_buf, bTAB], writes=[col_buf])
        Sc.add("dve", lambda e, o=col_ap, i1=tmp:
               e.tensor_tensor(o, o, i1, ALU.add),
               reads=[col_buf, bF32[0]], writes=[col_buf])
        gview = gam[:, gam_off:gam_off + NCH].unsqueeze(2).broadcast_to([128, NCH, D])
        Sc.add("dve", lambda e, o=out_ap, i0=col_ap, i1=gview:
               e.tensor_tensor(o.rearrange("p (c d) -> p c d", c=NCH),
                               i0.rearrange("p (c d) -> p c d", c=NCH),
                               i1, ALU.mult),
               reads=[col_buf, bGAM], writes=list(out_bufs))

    for n in range(NH):
        wkb, wka = bCOL[3 + (n % 2)], cs(3 + (n % 2))
        wvb, wva = bCOL[5 + (n % 2)], cs(5 + (n % 2))
        proj_tile(n, A["wk"], wkb, wka, cs(0), bCOL[0])
        transpose16(cs(0), bCOL[0], kt[:, n * T:(n + 1) * T], [bKT[n]])
        xpos_apply(kt[:, n * T:(n + 1) * T], bKT[n], NH * NCH + n * NCH,
                   kt[:, n * T:(n + 1) * T], [bKT[n]])
        proj_tile(n, A["wv"], wvb, wva, cs(1), bCOL[1])
        transpose16(cs(1), bCOL[1], cs(2), [bCOL[2]])
        dma("sp", A["vsp"][n], cs(2), reads=[bCOL[2]], writes=[bDR["vsp"]])

        def mk_state(n=n):
            def f(e):
                last = None
                for c_ in range(16):
                    last = nc.tensor.matmul(
                        pa[:, n * D:(n + 1) * D],
                        kt[:, n * T + c_ * 128:n * T + (c_ + 1) * 128],
                        cs(2)[:, c_ * 128:(c_ + 1) * 128],
                        start=(n % 4 == 0 and c_ == 0),
                        stop=(n % 4 == 3 and c_ == 15),
                        skip_group_check=True)
                return last
            return f
        Sc.add("te", mk_state(), reads=[bKT[n], bCOL[2]], writes=[bBank[n // 4]])

    Sc.add("act", lambda e, o=cs(7), i=pa[:, :]: e.copy(o, i),
           reads=banksA, writes=[bCOL[7]])
    dma("sp", A["ccin"][:, :], cs(7), reads=[bCOL[7]], writes=[bDR["ccin"]])
    Sc.add("pool", lambda e: e.collective_compute(
        "AllGather", ALU.bypass, [list(range(N_CORES))],
        [A["ccin"][:, :]], [A["ccout"][:, :]]),
        reads=[bDR["ccin"]], writes=[bDR["ccout"]])

    if stop_after == "p1":
        dbg_sbuf("d_kt", kt[:, :], list(bKT))
        dbg_dram("d_vsp", "vsp", [NH, C, T], BF16)
        dbg_dram("d_ccin", "ccin", [C, NH * D], BF16)
        return finish()

    dma("sp", tab[:, 0:T], A["cq"][:, :], reads=[bTAB], writes=[bTAB])
    dma("sp", tab[:, T:2 * T], A["sq"][:, :], reads=[bTAB], writes=[bTAB])
    for n in range(NH):
        wqb, wqa = bCOL[3 + (n % 2)], cs(3 + (n % 2))
        wgb, wga = bCOL[5 + (n % 2)], cs(5 + (n % 2))
        proj_tile(n, A["wq"], wqb, wqa, cs(0), bCOL[0])
        transpose16(cs(0), bCOL[0], cs(1), [bCOL[1]])
        xpos_apply(cs(1), bCOL[1], n * NCH, cs(1), [bCOL[1]])
        dma("sp", A["qsp"][n], cs(1), reads=[bCOL[1]], writes=[bDR["qsp"]])
        proj_tile(n, A["wg"], wgb, wga, cs(0), bCOL[0])
        transpose16(cs(0), bCOL[0], cs(2), [bCOL[2]])
        Sc.add("act", lambda e, o=cs(2): e.activation(o, o, AF.Silu),
               reads=[bCOL[2]], writes=[bCOL[2]])
        dma("sp", A["gsp"][n], cs(2), reads=[bCOL[2]], writes=[bDR["gsp"]])

    for r in range(N_CORES):
        dma("sp", cs(r), A["ccout"][r * C:(r + 1) * C, :],
            reads=[bDR["ccout"], bCOL[r]], writes=[bCOL[r]])
    for r in range(N_CORES):
        dma("sp", snb[:, :].rearrange("p (n d) -> p n d", n=NH),
            A["that"][r].rearrange("n p d -> p n d"),
            reads=[bSNB], writes=[bSNB])

        def mk_that(r=r):
            def f(e):
                last = None
                for n in range(NH):
                    last = nc.tensor.matmul(
                        pa[:, n * D:(n + 1) * D],
                        snb[:, n * D:(n + 1) * D],
                        cs(r)[:, n * D:(n + 1) * D],
                        start=(r == 0 and n % 4 == 0),
                        stop=(r == N_CORES - 1 and n % 4 == 3),
                        skip_group_check=True)
                return last
            return f
        Sc.add("te", mk_that(), reads=[bSNB, bCOL[r]], writes=banksA)
    Sc.add("act", lambda e, o=fs(0), i=pa[:, :]: e.copy(o, i),
           reads=banksA + [bF32[0]], writes=[bF32[0]])

    if stop_after == "exch":
        dbg_sbuf("d_sin", fs(0), [bF32[0]], dtype=F32, width=H)
        dbg_dram("d_qsp", "qsp", [NH, C, T], BF16)
        dbg_dram("d_gsp", "gsp", [NH, C, T], BF16)
        dbg_dram("d_ccout", "ccout", [N_CORES * C, NH * D], BF16)
        return finish()

    dma("sp", tabf[:, :], A["gnw"][:, :], reads=[bTAB], writes=[bTAB])
    dma("pool", gnbb[:, :], A["gnb"][:, :], writes=[bGNB])

    sinF = fs(0)
    for n in range(NH):
        vc, qc, gc = cs(0), cs(1), cs(2)
        ktT, qtT, atm, ztc = cs(3), cs(4), cs(5), cs(7)
        dma("sp", vc, A["vsp"][n], reads=[bDR["vsp"], bCOL[0]], writes=[bCOL[0]])
        dma("sp", qc, A["qsp"][n], reads=[bDR["qsp"], bCOL[1]], writes=[bCOL[1]])
        dma("act", gc, A["gsp"][n], reads=[bDR["gsp"], bCOL[2]], writes=[bCOL[2]])
        transpose16(kt[:, n * T:(n + 1) * T], bKT[n], ktT, [bCOL[3]], region="lo")
        transpose16(qc, bCOL[1], qtT, [bCOL[4]], region="lo")

        def mk_amm(ktT=ktT, qtT=qtT):
            def f(e):
                last = None
                for c_ in range(16):
                    last = nc.tensor.matmul(
                        pa[:, c_ * 128:(c_ + 1) * 128],
                        ktT[:, c_ * 128:(c_ + 1) * 128],
                        qtT[:, c_ * 128:(c_ + 1) * 128],
                        start=(c_ % 4 == 0), stop=(c_ % 4 == 3),
                        skip_group_check=True)
                return last
            return f
        Sc.add("te", mk_amm(), reads=[bCOL[3], bCOL[4]], writes=banksA)
        cau_b = cau[:, :].unsqueeze(1).broadcast_to([128, NCH, C])
        Sc.add("dve", lambda e, o=atm, i1=cau_b: e.tensor_tensor(
            o.rearrange("p (c a) -> p c a", c=NCH),
            pa[:, :].rearrange("p (c a) -> p c a", c=NCH), i1, ALU.mult),
               reads=banksA + [bCAU], writes=[bCOL[5]])

        def mk_ds(n=n, vc=vc):
            def f(e):
                last = None
                for c_ in range(16):
                    last = nc.tensor.matmul(
                        pa[:, c_ * 128:(c_ + 1) * 128],
                        kt[:, n * T + c_ * 128:n * T + (c_ + 1) * 128],
                        vc[:, c_ * 128:(c_ + 1) * 128],
                        start=(c_ % 4 == 0), stop=(c_ % 4 == 3),
                        skip_group_check=True)
                return last
            return f
        Sc.add("te", mk_ds(), reads=[bKT[n], bCOL[0]], writes=banksA)
        Sc.add("act", lambda e, o=fs(2), i=pa[:, :]: e.copy(o, i),
               reads=banksA, writes=[bF32[2]])

        Sc.add("dve", lambda e, n=n: e.tensor_copy(fs(1)[:, 0:128],
                                                   sinF[:, n * D:(n + 1) * D]),
               reads=[bF32[0]], writes=[bF32[1]])
        for c_ in range(15):
            Sc.add("dve", lambda e, c_=c_: e.tensor_add(
                fs(1)[:, (c_ + 1) * 128:(c_ + 2) * 128],
                fs(1)[:, c_ * 128:(c_ + 1) * 128],
                fs(2)[:, c_ * 128:(c_ + 1) * 128]),
                reads=[bF32[1], bF32[2]], writes=[bF32[1]])
        Sc.add("act", lambda e, o=snb[:, :], i=fs(1): e.copy(o, i),
               reads=[bF32[1]], writes=[bSNB])

        def mk_y(atm=atm, vc=vc, qtT=qtT):
            def f(e):
                last = None
                for c_ in range(16):
                    nc.tensor.matmul(
                        pa[:, c_ * 128:(c_ + 1) * 128],
                        atm[:, c_ * 128:(c_ + 1) * 128],
                        vc[:, c_ * 128:(c_ + 1) * 128],
                        start=(c_ % 4 == 0), stop=False, skip_group_check=True)
                    last = nc.tensor.matmul(
                        pa[:, c_ * 128:(c_ + 1) * 128],
                        qtT[:, c_ * 128:(c_ + 1) * 128],
                        snb[:, c_ * 128:(c_ + 1) * 128],
                        start=False, stop=(c_ % 4 == 3), skip_group_check=True)
                return last
            return f
        Sc.add("te", mk_y(), reads=[bCOL[5], bCOL[0], bCOL[4], bSNB], writes=banksA)
        Sc.add("act", lambda e, o=fs(2), i=pa[:, :]: e.copy(o, i),
               reads=banksA, writes=[bF32[2]])

        if stop_after == "p2bh" and n == DBGH:
            dbg_sbuf("d_ktT", ktT, [bCOL[3]], width=T)
            dbg_sbuf("d_qtT", qtT, [bCOL[4]], width=T)
            dbg_sbuf("d_atm", atm, [bCOL[5]], width=T)
            dbg_sbuf("d_snb", snb[:, :], [bSNB], width=T)
            dbg_sbuf("d_yraw", fs(2), [bF32[2]], dtype=F32, width=H)
            dbg_sbuf("d_vc", vc, [bCOL[0]], width=T)
            dbg_sbuf("d_kcol", kt[:, n * T:(n + 1) * T], [bKT[n]], width=T)
            return finish()

        sy = fs(2)
        sy3 = sy.rearrange("p (c d) -> p c d", c=NCH)
        Sc.add("dve", lambda e, o=sti(0), i=sy3: e.reduce_sum(o, i, axis=AX.X),
               reads=[bF32[2]], writes=[bST])
        Sc.add("act", lambda e, o=fs(1), i=sy: e.activation(o, i, AF.Square),
               reads=[bF32[2]], writes=[bF32[1]])
        Sc.add("dve", lambda e, o=sti(1),
               i=fs(1).rearrange("p (c d) -> p c d", c=NCH):
               e.reduce_sum(o, i, axis=AX.X),
               reads=[bF32[1]], writes=[bST])

        Sc.add("dve", lambda e: e.tensor_scalar_mul(sti(0), sti(0), 1.0 / D),
               reads=[bST], writes=[bST])
        Sc.add("dve", lambda e: e.tensor_scalar_mul(sti(1), sti(1), 1.0 / D),
               reads=[bST], writes=[bST])
        Sc.add("dve", lambda e: e.tensor_mul(sti(2), sti(0), sti(0)),
               reads=[bST], writes=[bST])
        Sc.add("dve", lambda e: e.tensor_sub(sti(1), sti(1), sti(2)),
               reads=[bST], writes=[bST])
        Sc.add("act", lambda e: e.activation(sti(2), sti(1), AF.Sqrt, bias=eps_col),
               reads=[bST], writes=[bST])
        Sc.add("dve", lambda e: e.reciprocal(sti(3), sti(2)), reads=[bST],
               writes=[bST])
        mu_b = sti(0).unsqueeze(2).broadcast_to([128, NCH, D])
        rs_b = sti(3).unsqueeze(2).broadcast_to([128, NCH, D])
        Sc.add("dve", lambda e, i1=mu_b: e.tensor_tensor(sy3, sy3, i1, ALU.subtract),
               reads=[bF32[2], bST], writes=[bF32[2]])
        Sc.add("dve", lambda e, i1=rs_b: e.tensor_tensor(sy3, sy3, i1, ALU.mult),
               reads=[bF32[2], bST], writes=[bF32[2]])
        gnw_b = tabf[:, n * D:(n + 1) * D].unsqueeze(1).broadcast_to([128, NCH, D])
        gnb_b = gnbb[:, n * D:(n + 1) * D].unsqueeze(1).broadcast_to([128, NCH, D])
        Sc.add("dve", lambda e, i1=gnw_b: e.tensor_tensor(sy3, sy3, i1, ALU.mult),
               reads=[bF32[2], bTAB], writes=[bF32[2]])
        Sc.add("dve", lambda e, i1=gnb_b: e.tensor_tensor(sy3, sy3, i1, ALU.add),
               reads=[bF32[2], bGNB], writes=[bF32[2]])
        zc = cs(6)
        Sc.add("dve", lambda e, o=zc, i0=sy, i1=gc: e.tensor_tensor(o, i0, i1,
                                                                    ALU.mult),
               reads=[bF32[2], bCOL[2]], writes=[bCOL[6]])
        transpose16(zc, bCOL[6], ztc, [bCOL[7]], region="lo")
        dma("sp", A["ztsp"][n], ztc, reads=[bCOL[7]], writes=[bDR["ztsp"]])

    if stop_after == "p2b":
        dbg_dram("d_ztsp", "ztsp", [NH, C, T], BF16)
        return finish()

    dma("pool", xt.rearrange("p (ht t) -> p ht t", ht=16),
        A["wo"].rearrange("(ht p) o -> p ht o", p=128),
        reads=list(bXT), writes=list(bXT))
    for c_ in range(NCH):
        ztc = cs(c_ % 2)
        dma("sp", ztc.rearrange("p (n a) -> p n a", n=NH),
            A["ztsp"][:, :, c_ * C:(c_ + 1) * C].rearrange("n p a -> p n a"),
            reads=[bDR["ztsp"], bCOL[c_ % 2]], writes=[bCOL[c_ % 2]])

        def mk_wo(ztc=ztc):
            def f(e):
                last = None
                for ht in range(16):
                    for j in range(4):
                        last = nc.tensor.matmul(
                            pa[:, j * 512:(j + 1) * 512],
                            ztc[:, ht * 128:(ht + 1) * 128],
                            xt[:, ht * T + j * 512:ht * T + (j + 1) * 512],
                            start=(ht == 0), stop=(ht == 15),
                            skip_group_check=True)
                return last
            return f
        Sc.add("te", mk_wo(), reads=[bCOL[c_ % 2]] + list(bXT), writes=banksA)
        xres = fs(0)
        dma("act", xres, A["x"][c_ * C:(c_ + 1) * C, :],
            reads=[bF32[0]], writes=[bF32[0]])
        y1 = fs(1)
        Sc.add("dve", lambda e, o=y1, i1=xres: e.tensor_add(o, pa[:, :], i1),
               reads=banksA + [bF32[0]], writes=[bF32[1]])
        Sc.add("dve", lambda e, o=st1(0), i=y1: e.reduce_sum(o, i, axis=AX.X),
               reads=[bF32[1]], writes=[bST])
        Sc.add("act", lambda e, o=fs(2), i=y1: e.activation(o, i, AF.Square),
               reads=[bF32[1]], writes=[bF32[2]])
        Sc.add("dve", lambda e, o=st1(1), i=fs(2): e.reduce_sum(o, i, axis=AX.X),
               reads=[bF32[2]], writes=[bST])

        Sc.add("dve", lambda e: e.tensor_scalar_mul(st1(0), st1(0), 1.0 / H),
               reads=[bST], writes=[bST])
        Sc.add("dve", lambda e: e.tensor_scalar_mul(st1(1), st1(1), 1.0 / H),
               reads=[bST], writes=[bST])
        Sc.add("dve", lambda e: e.tensor_mul(st1(2), st1(0), st1(0)),
               reads=[bST], writes=[bST])
        Sc.add("dve", lambda e: e.tensor_sub(st1(1), st1(1), st1(2)),
               reads=[bST], writes=[bST])
        Sc.add("act", lambda e: e.activation(st1(2), st1(1), AF.Sqrt, bias=eps_col),
               reads=[bST], writes=[bST])
        Sc.add("dve", lambda e: e.reciprocal(st1(3), st1(2)), reads=[bST],
               writes=[bST])
        Sc.add("dve", lambda e, o=y1: e.tensor_scalar(o, o, st1(0), st1(3),
                                                      ALU.subtract, ALU.mult),
               reads=[bF32[1], bST], writes=[bF32[1]])
        dma("act", A["y1sp"][c_], y1, reads=[bF32[1]], writes=[bDR["y1sp"]])
        y1b = cs(2)
        Sc.add("act", lambda e, o=y1b, i=y1: e.copy(o, i),
               reads=[bF32[1]], writes=[bCOL[2]])

        def mk_y1tp(y1b=y1b):
            def f(e):
                last = None
                for ht in range(16):
                    last = nc.tensor.matmul(
                        pbb[:, ht * 128:(ht + 1) * 128],
                        y1b[:, ht * 128:(ht + 1) * 128], idn[:, :],
                        start=(ht % 8 == 0), stop=(ht % 8 == 7),
                        is_transpose=True, skip_group_check=True)
                return last
            return f
        Sc.add("te", mk_y1tp(), reads=[bCOL[2], bIDN], writes=bTRlo)
        dstk = kt.rearrange("p (ht t) -> p ht t", ht=16)[:, :, c_ * C:(c_ + 1) * C]
        srck = pbb[:, 0:2048].rearrange("p (ht a) -> p ht a", ht=16)
        Sc.add("act", lambda e, o=dstk, i=srck: e.copy(o, i),
               reads=bTRlo, writes=list(bKT))

    if stop_after == "p3":
        dbg_dram("d_y1sp", "y1sp", [NCH, C, H], F32)
        dbg_sbuf("d_y1t", kt[:, :], list(bKT))
        return finish()

    dma("sp", tabf[:, :], A["sdown"][:, :], reads=[bTAB], writes=[bTAB])
    dma("sp", sgc[:, 0:INTER // C], A["sgate"][:, :], writes=[bSGC])
    dma("sp", sgc[:, INTER // C:], A["sup"][:, :], writes=[bSGC])

    out_dmas = []
    for s_ in range(NSTRIP):
        for tb in range(NTB):
            par = tb % 2
            wgs = cols[:, (par * 4) * T:(par * 4 + 2) * T]
            wus = cols[:, (par * 4 + 2) * T:(par * 4 + 4) * T]
            bwg = [bCOL[par * 4], bCOL[par * 4 + 1]]
            bwu = [bCOL[par * 4 + 2], bCOL[par * 4 + 3]]
            dma("pool", wgs.rearrange("p (h i) -> p h i", h=16),
                A["wgt"][tb].rearrange("h p i -> p h i"),
                reads=bwg, writes=bwg)
            dma("pool", wus.rearrange("p (h i) -> p h i", h=16),
                A["wut"][tb].rearrange("h p i -> p h i"),
                reads=bwu, writes=bwu)
            for j in range(2):
                it = tb * 2 + j
                bank_g = bBank[(it % 2) * 2]
                bank_u = bBank[(it % 2) * 2 + 1]
                pg = pa[:, (it % 2) * 1024:(it % 2) * 1024 + 512]
                pu = pa[:, (it % 2) * 1024 + 512:(it % 2) * 1024 + 1024]

                def mk_gu(j=j, pg=pg, pu=pu, wgs=wgs, wus=wus, s_=s_):
                    def f(e):
                        last = None
                        for ht in range(16):
                            rhs = kt[:, ht * T + s_ * STRIP:ht * T + (s_ + 1) * STRIP]
                            nc.tensor.matmul(
                                pg, wgs[:, ht * NIB + j * 128:ht * NIB + (j + 1) * 128],
                                rhs, start=(ht == 0), stop=(ht == 15),
                                skip_group_check=True)
                            last = nc.tensor.matmul(
                                pu, wus[:, ht * NIB + j * 128:ht * NIB + (j + 1) * 128],
                                rhs, start=(ht == 0), stop=(ht == 15),
                                skip_group_check=True)
                        return last
                    return f
                Sc.add("te", mk_gu(), reads=bwg + bwu + list(bKT),
                       writes=[bank_g, bank_u])
                gb = gub[:, (it % 2) * STRIP:(it % 2 + 1) * STRIP]
                Sc.add("act", lambda e, o=gb, i=pg, it=it:
                       e.activation(o, i, AF.Silu, scale=sgc[:, it:it + 1]),
                       reads=[bank_g, bSGC], writes=[bGUB[it % 2]])
                fu = xt[:, it * STRIP:(it + 1) * STRIP]
                Sc.add("dve", lambda e, o=fu, i0=pu, it=it, gb=gb:
                       e.scalar_tensor_tensor(o, i0, sgc[:, 64 + it:64 + it + 1],
                                              gb, ALU.mult, ALU.mult),
                       reads=[bank_u, bGUB[it % 2], bSGC], writes=[bXT[it // 4]])
        for pr in range(2):
            for grp in range(16):
                par = grp % 2
                wds = cols[:, (par * 4) * T:(par * 4 + 4) * T]
                bwd = [bCOL[par * 4 + k] for k in range(4)]
                dma("pool", wds.rearrange("p (g h) -> p g h", g=4),
                    A["wdt"][grp * 512:(grp + 1) * 512, :].rearrange(
                        "(g p) h -> p g h", p=128),
                    reads=bwd, writes=bwd)

                def mk_down(grp=grp, wds=wds, pr=pr):
                    def f(e):
                        last = None
                        for g4 in range(4):
                            it = grp * 4 + g4
                            for cp in range(2):
                                tok0 = (pr * 2 + cp) * 128
                                psum = (pa, pb)[cp]
                                for jn in range(4):
                                    last = nc.tensor.matmul(
                                        psum[:, jn * 512:(jn + 1) * 512],
                                        xt[:, it * STRIP + tok0:it * STRIP + tok0 + 128],
                                        wds[:, g4 * 2048 + jn * 512:
                                            g4 * 2048 + (jn + 1) * 512],
                                        start=(it == 0), stop=(it == 63),
                                        skip_group_check=True)
                        return last
                    return f
                Sc.add("te", mk_down(), reads=bwd + list(bXT),
                       writes=banksA + banksB)
            for cp in range(2):
                c_ = s_ * 4 + pr * 2 + cp
                psum = (pa, pb)[cp]
                pbanks = (banksA, banksB)[cp]
                t0 = fs(0)
                Sc.add("dve", lambda e, o=t0, i0=psum:
                       e.tensor_tensor(o, i0[:, :], tabf[:, :], ALU.mult),
                       reads=pbanks + [bTAB, bF32[0]], writes=[bF32[0]])
                y1r = fs(1)
                dma("act", y1r, A["y1sp"][c_], reads=[bDR["y1sp"], bF32[1]],
                    writes=[bF32[1]])
                Sc.add("dve", lambda e, o=t0, i1=y1r: e.tensor_add(o, o, i1),
                       reads=[bF32[0], bF32[1]], writes=[bF32[0]])
                Sc.add("dve", lambda e, o=st1(0), i=t0: e.reduce_sum(o, i, axis=AX.X),
                       reads=[bF32[0]], writes=[bST])
                Sc.add("act", lambda e, o=fs(2), i=t0: e.activation(o, i, AF.Square),
                       reads=[bF32[0]], writes=[bF32[2]])
                Sc.add("dve", lambda e, o=st1(1), i=fs(2):
                       e.reduce_sum(o, i, axis=AX.X),
                       reads=[bF32[2]], writes=[bST])

                Sc.add("dve", lambda e: e.tensor_scalar_mul(st1(0), st1(0),
                                                            1.0 / H),
                       reads=[bST], writes=[bST])
                Sc.add("dve", lambda e: e.tensor_scalar_mul(st1(1), st1(1),
                                                            1.0 / H),
                       reads=[bST], writes=[bST])
                Sc.add("dve", lambda e: e.tensor_mul(st1(2), st1(0), st1(0)),
                       reads=[bST], writes=[bST])
                Sc.add("dve", lambda e: e.tensor_sub(st1(1), st1(1), st1(2)),
                       reads=[bST], writes=[bST])
                Sc.add("act", lambda e: e.activation(st1(2), st1(1), AF.Sqrt,
                                                     bias=eps_col),
                       reads=[bST], writes=[bST])
                Sc.add("dve", lambda e: e.reciprocal(st1(3), st1(2)),
                       reads=[bST], writes=[bST])
                Sc.add("dve", lambda e, o=t0: e.tensor_scalar(o, o, st1(0), st1(3),
                                                              ALU.subtract, ALU.mult),
                       reads=[bF32[0], bST], writes=[bF32[0]])
                out_dmas.append(
                    dma("act", A["y"][c_ * C:(c_ + 1) * C, :], t0,
                        reads=[bF32[0]], writes=[], dsem="out"))

    Sc.add("sp", lambda e: e.nop(), deps=out_dmas)

    return finish()



def kernel(L0_attn_X, L0_attn_W_Q, L0_attn_W_K, L0_attn_W_V, L0_attn_W_G,
           L0_attn_W_O, L0_attn_gn_weight, L0_attn_gn_bias, L0_ffn_w_gate,
           L0_ffn_w_up, L0_ffn_w_down, L0_ffn_s_gate, L0_ffn_s_up, L0_ffn_s_down):
    X = np.asarray(L0_attn_X)
    cst = _build_host_constants()

    cat = lambda w: np.ascontiguousarray(
        np.asarray(w).transpose(1, 0, 2).reshape(H, H).astype(np.float32))
    wq, wk, wv = cat(L0_attn_W_Q), cat(L0_attn_W_K), cat(L0_attn_W_V)
    wg = np.ascontiguousarray(np.asarray(L0_attn_W_G, np.float32))
    wo = np.ascontiguousarray(np.asarray(L0_attn_W_O, np.float32))
    gnw = np.ascontiguousarray(np.broadcast_to(
        np.asarray(L0_attn_gn_weight, np.float32), (C, H)))
    gnb = np.ascontiguousarray(np.broadcast_to(
        np.asarray(L0_attn_gn_bias, np.float32), (C, H)))
    wgt = np.ascontiguousarray(
        np.asarray(L0_ffn_w_gate).reshape(16, C, NTB, NIB).transpose(2, 0, 1, 3))
    wut = np.ascontiguousarray(
        np.asarray(L0_ffn_w_up).reshape(16, C, NTB, NIB).transpose(2, 0, 1, 3))
    wdt = np.ascontiguousarray(np.asarray(L0_ffn_w_down))
    sgate = np.ascontiguousarray(
        np.asarray(L0_ffn_s_gate, np.float32).reshape(INTER // C, C).T)
    sup = np.ascontiguousarray(
        np.asarray(L0_ffn_s_up, np.float32).reshape(INTER // C, C).T)
    sdown = np.ascontiguousarray(np.broadcast_to(
        np.asarray(L0_ffn_s_down, np.float32), (C, H)))

    shared = dict(wq=wq, wk=wk, wv=wv, wg=wg, wo=wo, gnw=gnw, gnb=gnb,
                  cq=cst["cq"], sq=cst["sq"], ck=cst["ck"], sk=cst["sk"],
                  gq=cst["gq"], gk=cst["gk"], causal=cst["causal"],
                  ident=cst["ident"], wgt=wgt, wut=wut, wdt=wdt,
                  sgate=sgate, sup=sup, sdown=sdown)

    in_maps = []
    for core in range(N_CORES):
        bq, qq = core // NQ, core % NQ
        m = dict(shared)
        m["x"] = np.ascontiguousarray(X[bq, qq * T:(qq + 1) * T, :].astype(np.float32))
        m["that"] = cst["that_by_core"][core]
        in_maps.append(m)

    nc = build_program()
    res = run_bass_kernel_spmd(nc, in_maps, core_ids=list(range(N_CORES)))

    out = np.empty((B, S, H), np.float32)
    for core in range(N_CORES):
        bq, qq = core // NQ, core % NQ
        out[bq, qq * T:(qq + 1) * T, :] = res.results[core]["y"]
    return out
